# version 01 (final)
# baseline (speedup 1.0000x reference)
import contextlib
import numpy as np
import ml_dtypes
import concourse.bass as bass
import concourse.mybir as mybir
from concourse.bass_utils import run_bass_kernel_spmd

F32 = mybir.dt.float32
BF16 = mybir.dt.bfloat16
I32 = mybir.dt.int32
AF = mybir.ActivationFunctionType
ALU = mybir.AluOpType
bf16 = ml_dtypes.bfloat16

NCORES = 8
D = 4096
TPC = 1024
S = 4096
B = 2
INTER = 11008
GU_N = 2 * INTER
GU_W = GU_N // 8
DN_GPC = 11
ROPE_TH = 10000.0

P_SEQ = [0, 7, 1, 6, 2, 5, 3, 4] * 2


def make_identity(g, ident):
    g.memset(ident[:], 0.0)
    g.affine_select(out=ident[:], in_=ident[:], compare_op=ALU.not_equal,
                    fill=1.0, base=0, pattern=[[-1, ident.shape[1]]], channel_multiplier=1)



def build_convert():
    nc = bass.Bass("TRN2", target_bir_lowering=False)
    ap = lambda *a, **k: nc.dram_tensor(*a, **k).ap()
    w0s = ap("w0s", [512, D], F32, kind="ExternalInput")
    masks = ap("masks", [512, D], I32, kind="ExternalInput")
    q0s = ap("q0s", [512, D], I32, kind="ExternalInput")
    s0s = ap("s0s", [512, 32], F32, kind="ExternalInput")
    z0s = ap("z0s", [512, 32], F32, kind="ExternalInput")
    wqs = ap("wqs", [512, D], F32, kind="ExternalInput")
    wks = ap("wks", [128, D], F32, kind="ExternalInput")
    wvs = ap("wvs", [128, D], F32, kind="ExternalInput")
    wos = ap("wos", [512, D], F32, kind="ExternalInput")
    qgus = ap("qgus", [512, GU_W], I32, kind="ExternalInput")
    zgus = ap("zgus", [4, GU_W], I32, kind="ExternalInput")
    sgus = ap("sgus", [4, GU_N], F32, kind="ExternalInput")
    qdns = ap("qdns", [DN_GPC * 128, 512], I32, kind="ExternalInput")
    zdns = ap("zdns", [DN_GPC, 512], I32, kind="ExternalInput")
    sdns = ap("sdns", [DN_GPC, D], F32, kind="ExternalInput")

    wm_bf = ap("wm_bf", [512, D], BF16, kind="ExternalOutput")
    wf0_bf = ap("wf0_bf", [512, D], BF16, kind="ExternalOutput")
    wq_bf = ap("wq_bf", [512, D], BF16, kind="ExternalOutput")
    wk_bf = ap("wk_bf", [128, D], BF16, kind="ExternalOutput")
    wv_bf = ap("wv_bf", [128, D], BF16, kind="ExternalOutput")
    wo_bf = ap("wo_bf", [512, D], BF16, kind="ExternalOutput")
    guT_bf = ap("guT_bf", [GU_N, 512], BF16, kind="ExternalOutput")
    dnT_bf = ap("dnT_bf", [D, DN_GPC * 128], BF16, kind="ExternalOutput")

    GU_NCH = [(nch, (2048 if nch < 10 else GU_N - 10 * 2048)) for nch in range(11)]

    with contextlib.ExitStack() as es:
        E = es.enter_context
        u32t = E(nc.sbuf_tensor("u32t", [128, 4], BF16))
        u32tt = E(nc.sbuf_tensor("u32tt", [4, 128], BF16))
        identb = E(nc.sbuf_tensor("identb", [128, 128], BF16))
        identf = E(nc.sbuf_tensor("identf", [128, 128], F32))
        sw0 = E(nc.sbuf_tensor("sw0", [128, D], F32))
        qbuf = E(nc.sbuf_tensor("qbuf", [128, D], I32))
        smb = E(nc.sbuf_tensor("smb", [128, D], BF16))
        bs512 = E(nc.sbuf_tensor("bs512", [4, 512], F32))
        bacc = E(nc.sbuf_tensor("bacc", [4, 128], F32))
        bthr = E(nc.sbuf_tensor("bthr", [4, 128], BF16))
        bexp = E(nc.sbuf_tensor("bexp", [4, D], BF16))
        wout = E(nc.sbuf_tensor("wout", [128, D], BF16))
        ss0 = E(nc.sbuf_tensor("ss0", [128, 32], F32))
        sz0 = E(nc.sbuf_tensor("sz0", [128, 32], F32))
        scst = E(nc.sbuf_tensor("scst", [128, D], BF16))
        sqg = E(nc.sbuf_tensor("sqg", [128, GU_W], I32))
        unp = E(nc.sbuf_tensor("unp", [128, GU_N], BF16))
        sch = E(nc.sbuf_tensor("sch", [4, 2048], F32))
        zgq = E(nc.sbuf_tensor("zgq", [4, GU_W], I32))
        zchi = E(nc.sbuf_tensor("zchi", [4, 2048], I32))
        zsch = E(nc.sbuf_tensor("zsch", [4, 2048], F32))
        sT = E(nc.sbuf_tensor("sT", [128, 172 * 4], F32))
        zsT = E(nc.sbuf_tensor("zsT", [128, 172 * 4], F32))
        stg = E(nc.sbuf_tensor("stg", [128, 2 * 128], BF16))
        sdq = E(nc.sbuf_tensor("sdq", [128, 512], I32))
        zdq = E(nc.sbuf_tensor("zdq", [DN_GPC, 512], I32))
        sdf = E(nc.sbuf_tensor("sdf", [DN_GPC, D], F32))
        zdchi = E(nc.sbuf_tensor("zdchi", [DN_GPC, 2048], I32))
        zsdch = E(nc.sbuf_tensor("zsdch", [DN_GPC, 2048], F32))
        sTd = E(nc.sbuf_tensor("sTd", [128, 32 * DN_GPC], F32))
        zsTd = E(nc.sbuf_tensor("zsTd", [128, 32 * DN_GPC], F32))
        pbs = E(nc.psum_tensor("pbs", [4, 512], F32))
        pem = E(nc.psum_tensor("pem", [128, 512], F32))
        ptp0 = E(nc.psum_tensor("ptp0", [128, 128], BF16))
        ptp1 = E(nc.psum_tensor("ptp1", [128, 128], BF16))
        ptsa = E(nc.psum_tensor("ptsa", [128, 16], F32))
        ptsb = E(nc.psum_tensor("ptsb", [128, 16], F32))

        CON = E(nc.semaphore("CON"))
        W0L = E(nc.semaphore("W0L")); W0V = E(nc.semaphore("W0V"))
        W0T = E(nc.semaphore("W0T")); W0S = E(nc.semaphore("W0S"))
        F0L = E(nc.semaphore("F0L")); F0V = E(nc.semaphore("F0V")); F0S = E(nc.semaphore("F0S"))
        CAL = E(nc.semaphore("CAL")); CAS = E(nc.semaphore("CAS"))
        GPL = E(nc.semaphore("GPL"))
        GPV = E(nc.semaphore("GPV")); GPT = E(nc.semaphore("GPT"))
        GML = E(nc.semaphore("GML")); GMV = E(nc.semaphore("GMV"))
        GMT = E(nc.semaphore("GMT")); GMS = E(nc.semaphore("GMS"))
        PPV = E(nc.semaphore("PPV"))
        block = E(nc.Block())

        W0_VD = 19

        @block.gpsimd
        def _(g):
            g.memset(u32t[:], 1.0)
            g.affine_select(out=u32t[:], in_=u32t[:], compare_op=ALU.is_ge,
                            fill=0.0, base=0, pattern=[[-32, 4]], channel_multiplier=1)
            g.affine_select(out=u32t[:], in_=u32t[:], compare_op=ALU.is_ge,
                            fill=0.0, base=31, pattern=[[32, 4]], channel_multiplier=-1)
            g.memset(u32tt[:], 1.0)
            g.affine_select(out=u32tt[:], in_=u32tt[:], compare_op=ALU.is_ge,
                            fill=0.0, base=0, pattern=[[1, 128]], channel_multiplier=-32)
            g.affine_select(out=u32tt[:], in_=u32tt[:], compare_op=ALU.is_ge,
                            fill=0.0, base=31, pattern=[[-1, 128]], channel_multiplier=32)
            make_identity(g, identb)
            make_identity(g, identf)
            g.engine_nop().then_inc(CON, 1)
            jobs = [(wqs, wq_bf, 4), (wks, wk_bf, 1), (wvs, wv_bf, 1), (wos, wo_bf, 4)]
            i = 0
            for src, dst, ntile in jobs:
                for t in range(ntile):
                    if i > 0:
                        g.wait_ge(CAS, 16 * i)
                    g.dma_start(out=scst[:], in_=src[t * 128:(t + 1) * 128, :]).then_inc(CAL, 16)
                    g.wait_ge(CAL, 16 * (i + 1))
                    g.dma_start(out=dst[t * 128:(t + 1) * 128, :], in_=scst[:]).then_inc(CAS, 16)
                    i += 1
            g.wait_ge(CAS, 160)

        @block.sync
        def _(sync):
            for t in range(4):
                if t > 0:
                    sync.wait_ge(W0V, W0_VD * t)
                sync.dma_start(out=sw0[:], in_=w0s[t * 128:(t + 1) * 128, :]).then_inc(W0L, 16)
                sync.dma_start(out=qbuf[:], in_=masks[t * 128:(t + 1) * 128, :]).then_inc(W0L, 16)
                sync.wait_ge(W0V, W0_VD * t + 19)
                sync.dma_start(out=wm_bf[t * 128:(t + 1) * 128, :], in_=wout[:]).then_inc(W0S, 16)
            for t in range(4):
                sync.wait_ge(W0V, W0_VD * 4) if t == 0 else sync.wait_ge(F0V, t)
                sync.dma_start(out=qbuf[:], in_=q0s[t * 128:(t + 1) * 128, :]).then_inc(F0L, 16)
                sync.dma_start(out=ss0[:], in_=s0s[t * 128:(t + 1) * 128, :]).then_inc(F0L, 16)
                sync.dma_start(out=sz0[:], in_=z0s[t * 128:(t + 1) * 128, :]).then_inc(F0L, 16)
                sync.wait_ge(F0V, t + 1)
                sync.dma_start(out=wf0_bf[t * 128:(t + 1) * 128, :], in_=wout[:]).then_inc(F0S, 16)
            sync.dma_start(out=zgq[:], in_=zgus).then_inc(GPL, 16)
            sync.dma_start(out=zdq[:], in_=zdns).then_inc(GPL, 16)
            sync.dma_start(out=sdf[:], in_=sdns).then_inc(GPL, 16)
            pre = 0
            for nch, ncol in GU_NCH:
                if nch > 0:
                    sync.wait_ge(GPT, 2 * pre)
                sync.dma_start(out=sch[:, 0:ncol],
                               in_=sgus[:, nch * 2048:nch * 2048 + ncol]).then_inc(GPL, 16)
                pre += ncol // 128
            for g in range(4):
                if g > 0:
                    sync.wait_ge(GMV, g * 173)
                sync.dma_start(out=sqg[:], in_=qgus[g * 128:(g + 1) * 128, :]).then_inc(GML, 16)
            for kt in range(DN_GPC):
                sync.wait_ge(GMV, 4 * 173 + kt * 33)
                sync.dma_start(out=sdq[:], in_=qdns[kt * 128:(kt + 1) * 128, :]).then_inc(GML, 16)

        @block.vector
        def _(v):
            for t in range(4):
                v.wait_ge(W0L, 32 * (t + 1))
                if t > 0:
                    v.wait_ge(W0S, 16 * t)
                v.tensor_copy(smb[:], qbuf[:]).then_inc(W0V, 1)
                for ch in range(8):
                    v.wait_ge(W0T, 16 * t + ch + 1)
                    v.tensor_copy(bs512[:], pbs[:])
                    b3 = bs512[:, :].rearrange("p (k r) -> p k r", r=32)
                    v.reduce_sum(bacc[:, ch * 16:(ch + 1) * 16].unsqueeze(2), b3,
                                 axis=mybir.AxisListType.X).then_inc(W0V, 1)
                v.tensor_single_scalar(bthr[:], bacc[:], 0.0, ALU.is_gt).then_inc(W0V, 1)
                v.wait_ge(W0V, 19 * t + 10)
                v.tensor_copy(bexp[:, :].rearrange("p (k r) -> p k r", r=32),
                              bthr[:, :].unsqueeze(2).broadcast_to([4, 128, 32])).then_inc(W0V, 1)
                for ch in range(8):
                    v.wait_ge(W0T, 16 * t + 8 + ch + 1)
                    v.scalar_tensor_tensor(wout[:, ch * 512:(ch + 1) * 512], pem[:], 1.0,
                                           sw0[:, ch * 512:(ch + 1) * 512],
                                           ALU.mult, ALU.mult).then_inc(W0V, 1)
            for t in range(4):
                v.wait_ge(F0L, 48 * (t + 1))
                v.wait_ge(W0S, 64) if t == 0 else v.wait_ge(F0S, 16 * t)
                for gch in range(32):
                    ins = v.tensor_scalar(wout[:, gch * 128:(gch + 1) * 128],
                                          qbuf[:, gch * 128:(gch + 1) * 128],
                                          ss0[:, gch:gch + 1], sz0[:, gch:gch + 1],
                                          ALU.mult, ALU.add)
                ins.then_inc(F0V, 1)
            gpv = 0
            for nch, ncol in GU_NCH:
                v.wait_ge(GPL, 48 + 16 * (nch + 1))
                wcnt = ncol // 8
                z3 = zchi[:, :].rearrange("p (w j) -> p j w", j=8)
                for j in range(8):
                    v.tensor_scalar(z3[:, j, 0:wcnt], zgq[:, nch * 256:nch * 256 + wcnt],
                                    4 * j, 15, ALU.logical_shift_right, ALU.bitwise_and)
                v.tensor_copy(zsch[:, 0:ncol], zchi[:, 0:ncol]).then_inc(PPV, 1)
                v.wait_ge(PPV, nch + 1)
                v.tensor_mul(zsch[:, 0:ncol], zsch[:, 0:ncol],
                             sch[:, 0:ncol]).then_inc(GPV, 1); gpv += 1
                ntile = ncol // 128
                for k in range(ntile):
                    v.wait_ge(GPT, 2 * k + 2 + 2 * sum((c // 128) for _, c in GU_NCH[:nch]))
                    gt = nch * 16 + k
                    v.tensor_copy(sT[:, gt * 4:gt * 4 + 4], ptsa[:, 0:4])
                    v.tensor_copy(zsT[:, gt * 4:gt * 4 + 4], ptsb[:, 0:4]).then_inc(GPV, 1); gpv += 1
            gu_tp = 2 * sum((c // 128) for _, c in GU_NCH)
            for nch in range(2):
                z3 = zdchi[:, :].rearrange("p (w j) -> p j w", j=8)
                for j in range(8):
                    v.tensor_scalar(z3[:, j, :], zdq[:, nch * 256:(nch + 1) * 256],
                                    4 * j, 15, ALU.logical_shift_right, ALU.bitwise_and)
                v.tensor_copy(zsdch[:], zdchi[:]).then_inc(PPV, 1)
                v.wait_ge(PPV, 11 + nch + 1)
                v.tensor_mul(zsdch[:], zsdch[:],
                             sdf[:, nch * 2048:(nch + 1) * 2048]).then_inc(GPV, 1); gpv += 1
                for k in range(16):
                    v.wait_ge(GPT, gu_tp + 32 * nch + 2 * k + 2)
                    gt = nch * 16 + k
                    v.tensor_copy(sTd[:, gt * DN_GPC:(gt + 1) * DN_GPC], ptsa[:, 0:DN_GPC])
                    v.tensor_copy(zsTd[:, gt * DN_GPC:(gt + 1) * DN_GPC],
                                  ptsb[:, 0:DN_GPC]).then_inc(GPV, 1); gpv += 1
            u3 = unp[:, :].rearrange("p (w j) -> p j w", j=8)
            for g in range(4):
                v.wait_ge(GML, 16 * (g + 1))
                for j in range(8):
                    v.tensor_scalar(qbuf[:, 0:GU_W], sqg[:], 4 * j, 15,
                                    ALU.logical_shift_right, ALU.bitwise_and)
                    ins = v.tensor_copy(u3[:, j, :], qbuf[:, 0:GU_W])
                ins.then_inc(GMV, 1)
                for nt in range(172):
                    slot = nt % 2
                    v.wait_ge(GMT, g * 172 + nt + 1)
                    if g * 172 + nt >= 2:
                        v.wait_ge(GMS, 16 * (g * 172 + nt - 1))
                    v.tensor_scalar(stg[:, slot * 128:(slot + 1) * 128],
                                    (ptp0 if slot == 0 else ptp1)[:, :],
                                    sT[:, nt * 4 + g:nt * 4 + g + 1],
                                    zsT[:, nt * 4 + g:nt * 4 + g + 1],
                                    ALU.mult, ALU.subtract).then_inc(GMV, 1)
            for kt in range(DN_GPC):
                v.wait_ge(GML, 64 + 16 * (kt + 1))
                for j in range(8):
                    v.tensor_scalar(qbuf[:, 0:512], sdq[:], 4 * j, 15,
                                    ALU.logical_shift_right, ALU.bitwise_and)
                    ins = v.tensor_copy(u3[:, j, 0:512], qbuf[:, 0:512])
                ins.then_inc(GMV, 1)
                for nt in range(32):
                    slot = nt % 2
                    v.wait_ge(GMT, 4 * 172 + kt * 32 + nt + 1)
                    v.wait_ge(GMS, 16 * (688 + kt * 32 + nt - 1))
                    v.tensor_scalar(stg[:, slot * 128:(slot + 1) * 128],
                                    (ptp0 if slot == 0 else ptp1)[:, :],
                                    sTd[:, nt * DN_GPC + kt:nt * DN_GPC + kt + 1],
                                    zsTd[:, nt * DN_GPC + kt:nt * DN_GPC + kt + 1],
                                    ALU.mult, ALU.subtract).then_inc(GMV, 1)

        @block.tensor
        def _(t_):
            t_.wait_ge(CON, 1)
            for t in range(4):
                t_.wait_ge(W0V, W0_VD * t + 1)
                for ch in range(8):
                    if ch > 0:
                        t_.wait_ge(W0V, W0_VD * t + 1 + ch)
                    t_.matmul(pbs[:], u32t[:], smb[:, ch * 512:(ch + 1) * 512],
                              start=True, stop=True).then_inc(W0T, 1)
                for ch in range(8):
                    t_.wait_ge(W0V, W0_VD * t + 11 + ch)
                    t_.matmul(pem[:], u32tt[:], bexp[:, ch * 512:(ch + 1) * 512],
                              start=True, stop=True).then_inc(W0T, 1)
            gpv = 0
            for nch, ncol in GU_NCH:
                ntile = ncol // 128
                t_.wait_ge(GPV, gpv + 1)
                for k in range(ntile):
                    if k > 0:
                        t_.wait_ge(GPV, gpv + 1 + k)
                    t_.transpose(ptsa[:, 0:4],
                                 sch[:, k * 128:(k + 1) * 128],
                                 identf[0:4, 0:4]).then_inc(GPT, 1)
                    t_.transpose(ptsb[:, 0:4],
                                 zsch[:, k * 128:(k + 1) * 128],
                                 identf[0:4, 0:4]).then_inc(GPT, 1)
                gpv += 1 + ntile
            for nch in range(2):
                t_.wait_ge(GPV, gpv + 1)
                for k in range(16):
                    if k > 0:
                        t_.wait_ge(GPV, gpv + 1 + k)
                    t_.transpose(ptsa[:, 0:DN_GPC],
                                 sdf[:, nch * 2048 + k * 128:nch * 2048 + (k + 1) * 128],
                                 identf[0:DN_GPC, 0:DN_GPC]).then_inc(GPT, 1)
                    t_.transpose(ptsb[:, 0:DN_GPC],
                                 zsdch[:, k * 128:(k + 1) * 128],
                                 identf[0:DN_GPC, 0:DN_GPC]).then_inc(GPT, 1)
                gpv += 1 + 16
            for g in range(4):
                t_.wait_ge(GMV, g * 173 + 1)
                for nt in range(172):
                    slot = nt % 2
                    if nt >= 2:
                        t_.wait_ge(GMV, g * 173 + 1 + nt - 1)
                    t_.transpose((ptp0 if slot == 0 else ptp1)[:, :],
                                 unp[:, nt * 128:(nt + 1) * 128],
                                 identb[:]).then_inc(GMT, 1)
            for kt in range(DN_GPC):
                t_.wait_ge(GMV, 4 * 173 + kt * 33 + 1)
                for nt in range(32):
                    slot = nt % 2
                    if nt >= 2:
                        t_.wait_ge(GMV, 4 * 173 + kt * 33 + 1 + nt - 1)
                    t_.transpose((ptp0 if slot == 0 else ptp1)[:, :],
                                 unp[:, nt * 128:(nt + 1) * 128],
                                 identb[:]).then_inc(GMT, 1)

        @block.scalar
        def _(sc):
            for g in range(4):
                for nt in range(172):
                    slot = nt % 2
                    sc.wait_ge(GMV, g * 173 + 1 + nt + 1)
                    sc.dma_start(out=guT_bf[nt * 128:(nt + 1) * 128, g * 128:(g + 1) * 128],
                                 in_=stg[:, slot * 128:(slot + 1) * 128]).then_inc(GMS, 16)
            for kt in range(DN_GPC):
                for nt in range(32):
                    slot = nt % 2
                    sc.wait_ge(GMV, 4 * 173 + kt * 33 + 1 + nt + 1)
                    sc.dma_start(out=dnT_bf[nt * 128:(nt + 1) * 128, kt * 128:(kt + 1) * 128],
                                 in_=stg[:, slot * 128:(slot + 1) * 128]).then_inc(GMS, 16)
            sc.wait_ge(GMS, 16 * (688 + 352))
            sc.wait_ge(W0S, 64)
            sc.wait_ge(F0S, 64)

    return nc



def build_block0qkv():
    import os
    LVL = int(os.environ.get('L2LVL', '6'))
    nc = bass.Bass("TRN2", target_bir_lowering=False)
    ap = lambda *a, **k: nc.dram_tensor(*a, **k).ap()
    xT = ap("xT", [D, TPC], F32, kind="ExternalInput")
    wm = ap("wm", [D, D], BF16, kind="ExternalInput")
    wf = ap("wf", [D, D], BF16, kind="ExternalInput")
    b0r = ap("b0r", [128, 32], F32, kind="ExternalInput")
    wq = ap("wq", [D, D], BF16, kind="ExternalInput")
    wkk = ap("wkk", [1024, D], BF16, kind="ExternalInput")
    wvv = ap("wvv", [1024, D], BF16, kind="ExternalInput")
    pos = ap("pos", [1, TPC], I32, kind="ExternalInput")

    x1T = ap("x1T", [D, TPC], BF16, kind="ExternalOutput")
    xn1T = ap("xn1T", [D, TPC], BF16, kind="ExternalOutput")
    qT = ap("qT", [D, TPC], BF16, kind="ExternalOutput")
    kT = ap("kT", [1024, TPC], BF16, kind="ExternalOutput")
    vtok = ap("vtok", [TPC, 1024], BF16, kind="ExternalOutput")

    TWO_PI = float(2.0 * np.pi)
    PI = float(np.pi)
    NB_G = 8

    qkv_groups = []
    for kind, n in (("q", 32), ("k", 8), ("v", 8)):
        i = 0
        while i < n:
            g = 2 if (n - i == 4 or n - i == 2) else 3
            qkv_groups.append((kind, i, g))
            i += g
    import os as _os
    _lvl = int(_os.environ.get('L2LVL', '6'))
    MMONLY = _os.environ.get('L2MMONLY', '0') == '1'
    EVL = int(_os.environ.get('L2EVAC', '4'))
    if _lvl == 4:
        qkv_groups = [gg for gg in qkv_groups if gg[0] == "q"]
    elif _lvl == 5:
        qkv_groups = [gg for gg in qkv_groups if gg[0] in ("q", "k")]
    qk_before = []
    v_before = []
    nqk = nv = 0
    for kind, i0, gsz in qkv_groups:
        qk_before.append(nqk)
        v_before.append(nv)
        if kind == "v":
            nv += gsz * 2
        else:
            nqk += gsz * 2

    with contextlib.ExitStack() as es:
        E = es.enter_context
        identb = E(nc.sbuf_tensor("identb", [128, 128], BF16))
        ones128 = E(nc.sbuf_tensor("ones128", [128, 1], BF16))
        onesrow = E(nc.sbuf_tensor("onesrow", [1, 128], F32))
        epsr = E(nc.sbuf_tensor("epsr", [1, 1], F32))
        xb = E(nc.sbuf_tensor("xb", [128, 32 * TPC], BF16))
        xn = E(nc.sbuf_tensor("xn", [128, 32 * TPC], BF16))
        xsq = E(nc.sbuf_tensor("xsq", [128, TPC], BF16))
        b0sb = E(nc.sbuf_tensor("b0sb", [128, 32], F32))
        mrow = E(nc.sbuf_tensor("mrow", [1, TPC], F32))
        qrow = E(nc.sbuf_tensor("qrow", [1, TPC], F32))
        vrow = E(nc.sbuf_tensor("vrow", [1, TPC], F32))
        rrow = E(nc.sbuf_tensor("rrow", [1, TPC], F32))
        mbs = E(nc.sbuf_tensor("mbs", [128, TPC], F32))
        rbs = E(nc.sbuf_tensor("rbs", [128, TPC], F32))
        xnsc = E(nc.sbuf_tensor("xnsc", [128, TPC], F32))
        wsla = E(nc.sbuf_tensor("wsla", [128, 2 * 512], BF16))
        wslb = E(nc.sbuf_tensor("wslb", [128, 2 * 512], BF16))
        wslq = E(nc.sbuf_tensor("wslq", [128, 2 * 384], BF16))
        qsb = E(nc.sbuf_tensor("qsb", [128, 4 * 512], BF16))
        vsb = E(nc.sbuf_tensor("vsb", [128, 512], BF16))
        vtb = E(nc.sbuf_tensor("vtb", [128, 4 * 128], BF16))
        ps = [E(nc.psum_tensor(f"ps{i}", [128, 512], F32)) for i in range(8)]
        pu0 = ps[6]
        pu1 = ps[7][:, :].bitcast(BF16)[:, 0:128]

        CON = E(nc.semaphore("CON"))
        XL = E(nc.semaphore("XL"))
        PB = E(nc.semaphore("PB"))
        RPV = E(nc.semaphore("RPV"))
        RTM = E(nc.semaphore("RTM"))
        RP = E(nc.semaphore("RP"))
        SQ = E(nc.semaphore("SQ"))
        STM = E(nc.semaphore("STM"))
        LNV = E(nc.semaphore("LNV"))
        LNS = E(nc.semaphore("LNS"))
        BCT = E(nc.semaphore("BCT"))
        BCV = E(nc.semaphore("BCV"))
        XNV = E(nc.semaphore("XNV"))
        WA = [E(nc.semaphore(f"WA{i}")) for i in range(2)]
        WB = [E(nc.semaphore(f"WB{i}")) for i in range(2)]
        WQSm = [E(nc.semaphore(f"WQS{i}")) for i in range(2)]
        WC = E(nc.semaphore("WC"))
        WCQ = E(nc.semaphore("WCQ"))
        EVV = E(nc.semaphore("EVV"))
        EVS = E(nc.semaphore("EVS"))
        RMM = E(nc.semaphore("RMM"))
        U1V = E(nc.semaphore("U1V"))
        U2V = E(nc.semaphore("U2V"))
        QEV = E(nc.semaphore("QEV"))
        QST = E(nc.semaphore("QST"))
        VS = E(nc.semaphore("VS"))
        VT = E(nc.semaphore("VT"))
        VV = E(nc.semaphore("VV"))
        VST = E(nc.semaphore("VST"))
        OUT = E(nc.semaphore("OUT"))
        PPV = E(nc.semaphore("PPV"))
        PPS = E(nc.semaphore("PPS"))
        block = E(nc.Block())

        @block.gpsimd
        def _(g):
            make_identity(g, identb)
            g.memset(ones128[:], 1.0)
            g.memset(onesrow[:], 1.0)
            g.memset(epsr[:], 1e-5)
            g.engine_nop().then_inc(CON, 1)
            for c in range(32):
                g.dma_start(out=xb[:, c * TPC:(c + 1) * TPC],
                            in_=xT[c * 128:(c + 1) * 128, :]).then_inc(XL, 16)
            if LVL >= 2:
                g.wait_ge(EVV, NB_G * 8)
            else:
                g.wait_ge(XNV, 32)
            for c in range(32):
                g.dma_start(out=x1T[c * 128:(c + 1) * 128, :],
                            in_=xb[:, c * TPC:(c + 1) * TPC]).then_inc(OUT, 16)
            for c in range(32):
                g.wait_ge(XNV, (32 if LVL >= 3 else 0) + c + 1)
                g.dma_start(out=xn1T[c * 128:(c + 1) * 128, :],
                            in_=xn[:, c * TPC:(c + 1) * TPC]).then_inc(OUT, 16)
            if LVL >= 4 and not MMONLY:
                e = 0
                ve = 0
                for gi, (kind, i0, gsz) in enumerate(qkv_groups):
                    for ci in range(gsz):
                        cglob = i0 + ci
                        for th in range(2):
                            if kind == "v":
                                for st in range(4):
                                    g.wait_ge(VV, 4 * ve + st + 1)
                                    g.dma_start(
                                        out=vtok[th * 512 + st * 128:th * 512 + (st + 1) * 128,
                                                 cglob * 128:(cglob + 1) * 128],
                                        in_=vtb[:, st * 128:(st + 1) * 128]).then_inc(VST, 16)
                                ve += 1
                            else:
                                g.wait_ge(EVS, e + 1)
                                dst = qT if kind == "q" else kT
                                g.dma_start(
                                    out=dst[cglob * 128:(cglob + 1) * 128,
                                            th * 512:(th + 1) * 512],
                                    in_=qsb[:, (e % 4) * 512:(e % 4 + 1) * 512]).then_inc(QST, 16)
                                e += 1
                if nqk:
                    g.wait_ge(QST, 16 * nqk)
                if nv:
                    g.wait_ge(VST, 16 * 4 * (nv // 2))
            g.wait_ge(OUT, 64 * 16)

        @block.sync
        def _(sync):
            sync.dma_start(out=b0sb[:], in_=b0r).then_inc(PB, 16)
            sync.dma_start(out=b0sb[:], in_=b0r).then_inc(PB, 16)
            for ng in range(NB_G if LVL >= 2 else 0):
                for d in range(32):
                    i = ng * 32 + d
                    slot = i % 2
                    if i >= 2:
                        sync.wait_ge(WC, i - 1)
                    sync.dma_start_transpose(
                        wsla[:, slot * 512:(slot + 1) * 512],
                        wm[ng * 512:(ng + 1) * 512, d * 128:(d + 1) * 128]).then_inc(WA[slot], 16)
                    sync.dma_start_transpose(
                        wslb[:, slot * 512:(slot + 1) * 512],
                        wf[ng * 512:(ng + 1) * 512, d * 128:(d + 1) * 128]).then_inc(WB[slot], 16)
            for gi, (kind, i0, gsz) in enumerate(qkv_groups if LVL >= 4 else []):
                src = {"q": wq, "k": wkk, "v": wvv}[kind]
                for d in range(32):
                    i = gi * 32 + d
                    slot = i % 2
                    if i >= 2:
                        sync.wait_ge(WCQ, i - 1)
                    sync.dma_start_transpose(
                        wslq[:, slot * 384:slot * 384 + gsz * 128],
                        src[i0 * 128:(i0 + gsz) * 128,
                            d * 128:(d + 1) * 128]).then_inc(WQSm[slot], 16)
        @block.scalar
        def _(sc):
            for c in range(32):
                sc.wait_ge(XL, 16 * (c + 1))
                if c > 0:
                    sc.wait_ge(STM, 4 * c)
                sc.activation(xsq[:], xb[:, c * TPC:(c + 1) * TPC],
                              AF.Square).then_inc(SQ, 1)
            sc.wait_ge(LNV, 1)
            sc.activation(rrow[:], vrow[:], AF.Ln, bias=epsr[0:1, 0:1]).then_inc(PPS, 1)
            sc.wait_ge(PPS, 1)
            sc.activation(rrow[:], rrow[:], AF.Exp, scale=-0.5).then_inc(LNS, 1)
            for c in range(32 if LVL >= 3 else 0):
                sc.wait_ge(EVV, 8 * (c // 4 + 1))
                sc.wait_ge(STM, 128 + 4 * c)
                sc.activation(xsq[:], xb[:, c * TPC:(c + 1) * TPC],
                              AF.Square).then_inc(SQ, 1)
            if LVL >= 3:
                sc.wait_ge(LNV, 2)
                sc.activation(rrow[:], vrow[:], AF.Ln, bias=epsr[0:1, 0:1]).then_inc(PPS, 1)
                sc.wait_ge(PPS, 2)
                sc.activation(rrow[:], rrow[:], AF.Exp, scale=-0.5).then_inc(LNS, 1)
            e = 0
            ve = 0
            for gi, (kind, i0, gsz) in enumerate(qkv_groups if (LVL >= 4 and not MMONLY) else []):
                for ci in range(gsz):
                    for th in range(2):
                        bank = ci * 2 + th
                        sc.wait_ge(WCQ, 32 * (gi + 1))
                        if kind == "v":
                            if ve > 0:
                                sc.wait_ge(VT, 4 * ve)
                            sc.activation(vsb[:], ps[bank][:], AF.Copy).then_inc(VS, 1)
                            ve += 1
                        else:
                            if e > 3:
                                sc.wait_ge(QST, 16 * (e - 3))
                            sc.activation(qsb[:, (e % 4) * 512:(e % 4 + 1) * 512],
                                          ps[bank][:], AF.Copy).then_inc(EVS, 1)
                            e += 1

        @block.vector
        def _(v):
            v.wait_ge(PB, 16)
            v.wait_ge(CON, 1)
            v.wait_ge(STM, 128)
            v.tensor_scalar(mrow[:, 0:512], ps[0][0:1, 0:512], float(1.0 / D), None, ALU.mult)
            v.tensor_scalar(mrow[:, 512:1024], ps[1][0:1, 0:512], float(1.0 / D), None, ALU.mult)
            v.tensor_scalar(qrow[:, 0:512], ps[2][0:1, 0:512], float(1.0 / D), None, ALU.mult)
            v.tensor_scalar(qrow[:, 512:1024], ps[3][0:1, 0:512], float(1.0 / D), None, ALU.mult).then_inc(PPV, 1)
            v.wait_ge(PPV, 1)
            v.tensor_mul(vrow[:], mrow[:], mrow[:]).then_inc(PPV, 1)
            v.wait_ge(PPV, 2)
            v.tensor_sub(vrow[:], qrow[:], vrow[:]).then_inc(LNV, 1)
            for i in range(4):
                v.wait_ge(BCT, i + 1)
                dst = (mbs if i < 2 else rbs)
                v.tensor_copy(dst[:, (i % 2) * 512:(i % 2 + 1) * 512],
                              pu0[:, 0:512]).then_inc(BCV, 1)
            for c in range(32):
                v.tensor_sub(xnsc[:], xb[:, c * TPC:(c + 1) * TPC], mbs[:])
                v.tensor_mul(xn[:, c * TPC:(c + 1) * TPC],
                             xnsc[:], rbs[:]).then_inc(XNV, 1)
            for ng in range(NB_G if LVL >= 2 else 0):
                v.wait_ge(WC, 32 * (ng + 1))
                for bank in range(8):
                    c = ng * 4 + bank // 2
                    th = bank % 2
                    v.scalar_tensor_tensor(
                        xb[:, c * TPC + th * 512:c * TPC + (th + 1) * 512],
                        ps[bank][:], b0sb[:, c:c + 1],
                        xb[:, c * TPC + th * 512:c * TPC + (th + 1) * 512],
                        ALU.add, ALU.add).then_inc(EVV, 1)
            if LVL < 3:
                return
            v.wait_ge(STM, 256)
            v.tensor_scalar(mrow[:, 0:512], ps[0][0:1, 0:512], float(1.0 / D), None, ALU.mult)
            v.tensor_scalar(mrow[:, 512:1024], ps[1][0:1, 0:512], float(1.0 / D), None, ALU.mult)
            v.tensor_scalar(qrow[:, 0:512], ps[2][0:1, 0:512], float(1.0 / D), None, ALU.mult)
            v.tensor_scalar(qrow[:, 512:1024], ps[3][0:1, 0:512], float(1.0 / D), None, ALU.mult).then_inc(PPV, 1)
            v.wait_ge(PPV, 3)
            v.tensor_mul(vrow[:], mrow[:], mrow[:]).then_inc(PPV, 1)
            v.wait_ge(PPV, 4)
            v.tensor_sub(vrow[:], qrow[:], vrow[:]).then_inc(LNV, 1)
            for i in range(4):
                v.wait_ge(BCT, 4 + i + 1)
                dst = (mbs if i < 2 else rbs)
                v.tensor_copy(dst[:, (i % 2) * 512:(i % 2 + 1) * 512],
                              pu0[:, 0:512]).then_inc(BCV, 1)
            for c in range(32):
                v.tensor_sub(xnsc[:], xb[:, c * TPC:(c + 1) * TPC], mbs[:])
                v.tensor_mul(xn[:, c * TPC:(c + 1) * TPC],
                             xnsc[:], rbs[:]).then_inc(XNV, 1)
            ve = 0
            for gi, (kind, i0, gsz) in enumerate(qkv_groups if (LVL >= 4 and not MMONLY) else []):
                for ci in range(gsz):
                    for th in range(2):
                        if kind == "v":
                            for st in range(4):
                                v.wait_ge(VT, 4 * ve + st + 1)
                                if ve > 0:
                                    v.wait_ge(VST, 16 * (4 * (ve - 1) + st + 1))
                                v.tensor_copy(vtb[:, st * 128:(st + 1) * 128],
                                              pu1).then_inc(VV, 1)
                            ve += 1

        @block.tensor
        def _(t_):
            t_.wait_ge(CON, 1)
            for c in range(32):
                t_.wait_ge(SQ, c + 1)
                t_.matmul(ps[0][0:1, 0:512], ones128[:],
                          xb[:, c * TPC:c * TPC + 512], start=(c == 0), stop=(c == 31)).then_inc(STM, 1)
                t_.matmul(ps[1][0:1, 0:512], ones128[:],
                          xb[:, c * TPC + 512:(c + 1) * TPC], start=(c == 0), stop=(c == 31)).then_inc(STM, 1)
                t_.matmul(ps[2][0:1, 0:512], ones128[:],
                          xsq[:, 0:512], start=(c == 0), stop=(c == 31)).then_inc(STM, 1)
                t_.matmul(ps[3][0:1, 0:512], ones128[:],
                          xsq[:, 512:1024], start=(c == 0), stop=(c == 31)).then_inc(STM, 1)
            t_.wait_ge(LNV, 1)
            t_.wait_ge(LNS, 1)
            for i in range(4):
                if i > 0:
                    t_.wait_ge(BCV, i)
                srcrow = (mrow if i < 2 else rrow)
                t_.matmul(pu0[:, 0:512], onesrow[0:1, :],
                          srcrow[0:1, (i % 2) * 512:(i % 2 + 1) * 512],
                          start=True, stop=True).then_inc(BCT, 1)
            if LVL < 2:
                return
            t_.wait_ge(XNV, 32)
            t_.wait_ge(BCV, 4)
            for ng in range(NB_G):
                if ng > 0:
                    t_.wait_ge(EVV, 8 * ng)
                for d in range(32):
                    i = ng * 32 + d
                    slot = i % 2
                    t_.wait_ge(WA[slot], 16 * (i // 2 + 1))
                    t_.wait_ge(WB[slot], 16 * (i // 2 + 1))
                    for n4 in range(4):
                        for th in range(2):
                            bank = n4 * 2 + th
                            t_.matmul(ps[bank][:],
                                      wsla[:, slot * 512 + n4 * 128:slot * 512 + (n4 + 1) * 128],
                                      xn[:, d * TPC + th * 512:d * TPC + (th + 1) * 512],
                                      start=(d == 0), stop=False)
                            mmf = t_.matmul(ps[bank][:],
                                      wslb[:, slot * 512 + n4 * 128:slot * 512 + (n4 + 1) * 128],
                                      xn[:, d * TPC + th * 512:d * TPC + (th + 1) * 512],
                                      start=False, stop=(d == 31))
                    mmf.then_inc(WC, 1)
            if LVL < 3:
                return
            t_.wait_ge(EVV, 64)
            for c in range(32):
                t_.wait_ge(SQ, 32 + c + 1)
                t_.matmul(ps[0][0:1, 0:512], ones128[:],
                          xb[:, c * TPC:c * TPC + 512], start=(c == 0), stop=(c == 31)).then_inc(STM, 1)
                t_.matmul(ps[1][0:1, 0:512], ones128[:],
                          xb[:, c * TPC + 512:(c + 1) * TPC], start=(c == 0), stop=(c == 31)).then_inc(STM, 1)
                t_.matmul(ps[2][0:1, 0:512], ones128[:],
                          xsq[:, 0:512], start=(c == 0), stop=(c == 31)).then_inc(STM, 1)
                t_.matmul(ps[3][0:1, 0:512], ones128[:],
                          xsq[:, 512:1024], start=(c == 0), stop=(c == 31)).then_inc(STM, 1)
            t_.wait_ge(LNV, 2)
            t_.wait_ge(LNS, 2)
            for i in range(4):
                t_.wait_ge(BCV, 4 + i)
                srcrow = (mrow if i < 2 else rrow)
                t_.matmul(pu0[:, 0:512], onesrow[0:1, :],
                          srcrow[0:1, (i % 2) * 512:(i % 2 + 1) * 512],
                          start=True, stop=True).then_inc(BCT, 1)
            if LVL < 4:
                return
            t_.wait_ge(XNV, 64)
            e = 0
            ve = 0
            for gi, (kind, i0, gsz) in enumerate(qkv_groups):
                if gi > 0 and not MMONLY:
                    if qk_before[gi] > 0:
                        t_.wait_ge(EVS, qk_before[gi])
                    if v_before[gi] > 0:
                        t_.wait_ge(VS, v_before[gi])
                for d in range(32):
                    i = gi * 32 + d
                    slot = i % 2
                    t_.wait_ge(WQSm[slot], 16 * (i // 2 + 1))
                    for ci in range(gsz):
                        for th in range(2):
                            bank = ci * 2 + th
                            mmf = t_.matmul(ps[bank][:],
                                      wslq[:, slot * 384 + ci * 128:slot * 384 + (ci + 1) * 128],
                                      xn[:, d * TPC + th * 512:d * TPC + (th + 1) * 512],
                                      start=(d == 0), stop=(d == 31))
                    mmf.then_inc(WCQ, 1)
                for ci in range(gsz if not MMONLY else 0):
                    for th in range(2):
                        if kind == "v":
                            t_.wait_ge(VS, ve + 1)
                            for st in range(4):
                                if 4 * ve + st > 0:
                                    t_.wait_ge(VV, 4 * ve + st)
                                t_.transpose(pu1, vsb[:, st * 128:(st + 1) * 128],
                                             identb[:]).then_inc(VT, 1)
                            ve += 1
                        else:
                            e += 1

    return nc



C_SHIFT = 12.0
ISQD = 0.08838834764831845


def build_attention():
    nc = bass.Bass("TRN2", target_bir_lowering=False)
    ap = lambda *a, **k: nc.dram_tensor(*a, **k).ap()
    qin = [ap(f"q{u}", [128, 2048], BF16, kind="ExternalInput") for u in range(16)]
    kin = [ap(f"k{u}", [128, 512 * (P_SEQ[u] + 1)], BF16, kind="ExternalInput") for u in range(16)]
    vin = [ap(f"v{u}", [128, 512 * (P_SEQ[u] + 1)], BF16, kind="ExternalInput") for u in range(16)]
    oout = [ap(f"o{u}", [128, 2048], BF16, kind="ExternalOutput") for u in range(16)]

    TWO_PI = float(2.0 * np.pi)
    PI = float(np.pi)

    with contextlib.ExitStack() as es:
        E = es.enter_context
        identb = E(nc.sbuf_tensor("identb", [128, 128], BF16))
        prot = E(nc.sbuf_tensor("prot", [128, 128], BF16))
        ones128 = E(nc.sbuf_tensor("ones128", [128, 1], BF16))
        onesrow = E(nc.sbuf_tensor("onesrow", [1, 128], F32))
        negc = E(nc.sbuf_tensor("negc", [128, 1], F32))
        dm = E(nc.sbuf_tensor("dm", [128, 4 * 512], BF16))
        ivfi = E(nc.sbuf_tensor("ivfi", [128, 1], I32))
        ivf2 = E(nc.sbuf_tensor("ivf2", [128, 1], F32))
        ivf = E(nc.sbuf_tensor("ivf", [128, 1], F32))
        posi = E(nc.sbuf_tensor("posi", [128, S], I32))
        ang = E(nc.sbuf_tensor("ang", [128, S], F32))
        rs1 = E(nc.sbuf_tensor("rs1", [128, S], F32))
        rs2 = E(nc.sbuf_tensor("rs2", [128, S], F32))
        cosT = E(nc.sbuf_tensor("cosT", [128, S], F32))
        sinT = E(nc.sbuf_tensor("sinT", [128, S], F32))
        qsl = E(nc.sbuf_tensor("qsl", [128, 2 * 2048], BF16))
        ksl = E(nc.sbuf_tensor("ksl", [128, 2 * 4096], BF16))
        vsl = E(nc.sbuf_tensor("vsl", [128, 2 * 4096], BF16))
        u1b = E(nc.sbuf_tensor("u1b", [128, 512], F32))
        u2b = E(nc.sbuf_tensor("u2b", [128, 512], F32))
        pt = E(nc.sbuf_tensor("pt", [128, 2 * 512], BF16))
        rrow = E(nc.sbuf_tensor("rrow", [1, 512], F32))
        rb = E(nc.sbuf_tensor("rb", [128, 512], F32))
        attst = E(nc.sbuf_tensor("attst", [128, 2 * 2048], BF16))
        pS0 = E(nc.psum_tensor("pS0", [128, 512], F32))
        pS1 = E(nc.psum_tensor("pS1", [128, 512], F32))
        pSm = E(nc.psum_tensor("pSm", [1, 512], F32))
        pAV = E(nc.psum_tensor("pAV", [128, 512], F32))
        pRB = E(nc.psum_tensor("pRB", [128, 512], F32))
        pu0 = E(nc.psum_tensor("pu0", [128, 512], F32))
        pu1 = E(nc.psum_tensor("pu1", [128, 512], F32))

        CON = E(nc.semaphore("CON"))
        RTS = E(nc.semaphore("RTS"))
        RTV = E(nc.semaphore("RTV"))
        QL = [E(nc.semaphore(f"QL{i}")) for i in range(2)]
        KL = [E(nc.semaphore(f"KL{i}")) for i in range(2)]
        VL = [E(nc.semaphore(f"VL{i}")) for i in range(2)]
        RPM = E(nc.semaphore("RPM"))
        RPU = E(nc.semaphore("RPU"))
        RPD = E(nc.semaphore("RPD"))
        SEXP = E(nc.semaphore("SEXP"))
        MSKV = E(nc.semaphore("MSKV"))
        PTC = E(nc.semaphore("PTC"))
        STT = E(nc.semaphore("STT"))
        RCP = E(nc.semaphore("RCP"))
        BCT = E(nc.semaphore("BCT"))
        BCV = E(nc.semaphore("BCV"))
        ATV = E(nc.semaphore("ATV"))
        UC = E(nc.semaphore("UC"))
        PPV = E(nc.semaphore("PPV"))
        PPS = E(nc.semaphore("PPS"))
        block = E(nc.Block())

        NKT = [4 * (P_SEQ[u] + 1) for u in range(16)]
        NRT = [(P_SEQ[u] + 1) + 4 for u in range(16)]
        rpm_b = [sum(NRT[:u]) for u in range(16)]
        kt_b = [4 * sum(NKT[:u]) for u in range(16)]

        @block.gpsimd
        def _(g):
            make_identity(g, identb)
            g.memset(prot[:], 0.0)
            g.affine_select(out=prot[:], in_=prot[:], compare_op=ALU.not_equal,
                            fill=-1.0, base=-64, pattern=[[-1, 128]], channel_multiplier=1)
            g.affine_select(out=prot[:], in_=prot[:], compare_op=ALU.not_equal,
                            fill=1.0, base=-64, pattern=[[1, 128]], channel_multiplier=-1)
            g.memset(ones128[:], 1.0)
            g.memset(onesrow[:], 1.0)
            g.memset(negc[:], -C_SHIFT)
            for j in range(4):
                g.memset(dm[:, j * 512:(j + 1) * 512], 1.0)
                g.affine_select(out=dm[:, j * 512:(j + 1) * 512],
                                in_=dm[:, j * 512:(j + 1) * 512],
                                compare_op=ALU.is_ge, fill=0.0,
                                base=-128 * j, pattern=[[1, 512]], channel_multiplier=-1)
            g.iota(ivfi[:], pattern=[[1, 1]], base=0, channel_multiplier=1)
            g.iota(posi[:], pattern=[[1, S]], base=0, channel_multiplier=0)
            g.engine_nop().then_inc(CON, 1)
            for u in range(16):
                g.wait_ge(ATV, 4 * (u + 1))
                g.dma_start(out=oout[u], in_=attst[:, (u % 2) * 2048:(u % 2 + 1) * 2048]).then_inc(UC, 16)
            g.wait_ge(UC, 16 * 16)

        @block.sync
        def _(sync):
            for u in range(16):
                slot = u % 2
                if u >= 2:
                    sync.wait_ge(UC, 16 * (u - 1))
                sync.dma_start(out=qsl[:, slot * 2048:slot * 2048 + 2048],
                               in_=qin[u]).then_inc(QL[slot], 16)
                sync.dma_start(out=ksl[:, slot * 4096:slot * 4096 + 512 * (P_SEQ[u] + 1)],
                               in_=kin[u]).then_inc(KL[slot], 16)
                sync.dma_start(out=vsl[:, slot * 4096:slot * 4096 + 512 * (P_SEQ[u] + 1)],
                               in_=vin[u]).then_inc(VL[slot], 16)

        @block.vector
        def _(v):
            v.wait_ge(CON, 1)
            v.tensor_scalar(ivfi[:], ivfi[:], 63, None, ALU.bitwise_and).then_inc(PPV, 1)
            v.wait_ge(PPV, 1)
            v.tensor_copy(ivf2[:], ivfi[:]).then_inc(RTV, 1)
            v.wait_ge(RTS, 1)
            v.tensor_copy(ang[:], posi[:])
            v.tensor_scalar(ang[:], ang[:], ivf[:, 0:1], None, ALU.mult)
            v.tensor_scalar(rs1[:], ang[:], float(1.0 / TWO_PI), None, ALU.mult)
            v.tensor_scalar(rs2[:], rs1[:], 8388608.0, None, ALU.add)
            v.tensor_scalar(rs2[:], rs2[:], -8388608.0, None, ALU.add)
            v.scalar_tensor_tensor(rs1[:], rs2[:], -TWO_PI, ang[:], ALU.mult, ALU.add)
            v.tensor_scalar(rs2[:], rs1[:], PI / 2, None, ALU.add)
            v.tensor_single_scalar(ang[:], rs2[:], PI, ALU.is_gt)
            v.scalar_tensor_tensor(rs2[:], ang[:], -TWO_PI, rs2[:],
                                   ALU.mult, ALU.add).then_inc(RTV, 1)
            v.wait_ge(RTS, 3)
            rpm = 0
            msk = 0
            for u in range(16):
                p = P_SEQ[u]
                slot = u % 2
                nkt = NKT[u]
                for j in range(NRT[u]):
                    isq = j >= (p + 1)
                    if isq:
                        base = slot * 2048 + (j - (p + 1)) * 512
                        buf = qsl
                        tslice = slice(512 * p, 512 * (p + 1))
                    else:
                        base = slot * 4096 + j * 512
                        buf = ksl
                        tslice = slice(512 * j, 512 * (j + 1))
                    v.wait_ge(RPM, rpm + 1)
                    v.tensor_mul(u1b[:], buf[:, base:base + 512],
                                 cosT[:, tslice])
                    v.tensor_mul(u2b[:], (pu0 if rpm % 2 == 0 else pu1)[:, :],
                                 sinT[:, tslice]).then_inc(RPU, 1)
                    v.tensor_add(buf[:, base:base + 512], u1b[:],
                                 u2b[:]).then_inc(RPD, 1)
                    rpm += 1
                if u >= 2:
                    v.wait_ge(UC, 16 * (u - 1))
                for qh in range(4):
                    gidx = kt_b[u] + qh * nkt
                    for kt in range(nkt):
                        if kt >= nkt - 4:
                            jj = kt - (nkt - 4)
                            v.wait_ge(SEXP, gidx + kt + 1)
                            v.tensor_mul(pt[:, (kt % 2) * 512:(kt % 2 + 1) * 512],
                                         pt[:, (kt % 2) * 512:(kt % 2 + 1) * 512],
                                         dm[:, jj * 512:(jj + 1) * 512]).then_inc(MSKV, 1)
                            msk += 1
                    v.wait_ge(BCT, 4 * u + qh + 1)
                    v.tensor_copy(rb[:], pRB[:]).then_inc(BCV, 1)
                    v.tensor_mul(attst[:, slot * 2048 + qh * 512:slot * 2048 + (qh + 1) * 512],
                                 pAV[:], rb[:]).then_inc(ATV, 1)

        @block.scalar
        def _(sc):
            sc.wait_ge(RTV, 1)
            sc.activation(ivf[:], ivf2[:], AF.Exp,
                          scale=float(-np.log(ROPE_TH) / 64.0)).then_inc(RTS, 1)
            sc.wait_ge(RTV, 2)
            sc.activation(sinT[:], rs1[:], AF.Sin).then_inc(RTS, 1)
            sc.activation(cosT[:], rs2[:], AF.Sin).then_inc(RTS, 1)
            for u in range(16):
                nkt = NKT[u]
                for qh in range(4):
                    gidx = kt_b[u] + qh * nkt
                    for kt in range(nkt):
                        sc.wait_ge(STT, gidx + kt + 1)
                        if gidx + kt >= 2:
                            sc.wait_ge(PTC, gidx + kt - 1)
                        sc.activation(pt[:, (kt % 2) * 512:(kt % 2 + 1) * 512],
                                      (pS0 if kt % 2 == 0 else pS1)[:, :],
                                      AF.Exp, bias=negc[:, 0:1],
                                      scale=ISQD).then_inc(SEXP, 1)
                    sc.wait_ge(PTC, gidx + nkt)
                    sc.activation(rrow[:], pSm[0:1, :], AF.Ln).then_inc(PPS, 1)
                    sc.wait_ge(PPS, 4 * u + qh + 1)
                    sc.activation(rrow[:], rrow[:], AF.Exp, scale=-1.0).then_inc(RCP, 1)

        @block.tensor
        def _(t_):
            t_.wait_ge(CON, 1)
            rpm = 0
            for u in range(16):
                p = P_SEQ[u]
                slot = u % 2
                nkt = NKT[u]
                for j in range(NRT[u]):
                    isq = j >= (p + 1)
                    if isq:
                        base = slot * 2048 + (j - (p + 1)) * 512
                        buf = qsl
                        if j == p + 1:
                            t_.wait_ge(QL[slot], 16 * (u // 2 + 1))
                    else:
                        base = slot * 4096 + j * 512
                        buf = ksl
                        if j == 0:
                            t_.wait_ge(KL[slot], 16 * (u // 2 + 1))
                    if rpm >= 2:
                        t_.wait_ge(RPU, rpm - 1)
                    t_.matmul((pu0 if rpm % 2 == 0 else pu1)[:, :], prot[:],
                              buf[:, base:base + 512],
                              start=True, stop=True).then_inc(RPM, 1)
                    rpm += 1
                t_.wait_ge(VL[slot], 16 * (u // 2 + 1))
                t_.wait_ge(RPD, rpm)
                for qh in range(4):
                    gidx = kt_b[u] + qh * nkt
                    for kt in range(nkt):
                        if gidx + kt >= 2:
                            t_.wait_ge(SEXP, gidx + kt - 1)
                        t_.matmul((pS0 if kt % 2 == 0 else pS1)[:, :],
                                  ksl[:, slot * 4096 + kt * 128:slot * 4096 + (kt + 1) * 128],
                                  qsl[:, slot * 2048 + qh * 512:slot * 2048 + (qh + 1) * 512],
                                  start=True, stop=True).then_inc(STT, 1)
                        if kt >= nkt - 4:
                            t_.wait_ge(MSKV, 16 * u + 4 * qh + (kt - nkt + 4) + 1)
                        else:
                            t_.wait_ge(SEXP, gidx + kt + 1)
                        t_.matmul(pSm[:], ones128[:],
                                  pt[:, (kt % 2) * 512:(kt % 2 + 1) * 512],
                                  start=(kt == 0), stop=(kt == nkt - 1))
                        t_.matmul(pAV[:],
                                  vsl[:, slot * 4096 + kt * 128:slot * 4096 + (kt + 1) * 128],
                                  pt[:, (kt % 2) * 512:(kt % 2 + 1) * 512],
                                  start=(kt == 0), stop=(kt == nkt - 1)).then_inc(PTC, 1)
                    t_.wait_ge(RCP, 4 * u + qh + 1)
                    if 4 * u + qh >= 1:
                        t_.wait_ge(BCV, 4 * u + qh)
                    t_.matmul(pRB[:], onesrow[0:1, :], rrow[0:1, :],
                              start=True, stop=True).then_inc(BCT, 1)
                    t_.wait_ge(ATV, 4 * u + qh + 1)

    return nc



def build_final():
    nc = bass.Bass("TRN2", target_bir_lowering=False)
    ap = lambda *a, **k: nc.dram_tensor(*a, **k).ap()
    x1T = ap("x1T", [D, TPC], BF16, kind="ExternalInput")
    xn1T = ap("xn1T", [D, TPC], BF16, kind="ExternalInput")
    attT = ap("attT", [D, TPC], BF16, kind="ExternalInput")
    wo = ap("wo", [D, D], BF16, kind="ExternalInput")
    guT = ap("guT", [GU_N, D], BF16, kind="ExternalInput")
    dnT = ap("dnT", [D, INTER], BF16, kind="ExternalInput")
    x2T = ap("x2T", [D, TPC], F32, kind="ExternalOutput")
    hdr = ap("hdr", [INTER, TPC], BF16, kind="Internal")

    NGRP = 43
    NI = 86

    with contextlib.ExitStack() as es:
        E = es.enter_context
        xn = E(nc.sbuf_tensor("xn", [128, 32 * TPC], BF16))
        hbuf = E(nc.sbuf_tensor("hbuf", [128, NI * 512], BF16))
        attb = E(nc.sbuf_tensor("attb", [128, 32 * 512], BF16))
        gsl = E(nc.sbuf_tensor("gsl", [128, 2 * 256], BF16))
        usl = E(nc.sbuf_tensor("usl", [128, 2 * 256], BF16))
        wsl = E(nc.sbuf_tensor("wsl", [128, 2 * 512], BF16))
        sg = E(nc.sbuf_tensor("sg", [128, 2 * 512], BF16))
        gsb = E(nc.sbuf_tensor("gsb", [128, 512], F32))
        hst = E(nc.sbuf_tensor("hst", [128, 4 * 512], BF16))
        x1t = E(nc.sbuf_tensor("x1t", [128, 4 * 512], BF16))
        x2st = E(nc.sbuf_tensor("x2st", [128, 2 * 512], F32))
        ps = [E(nc.psum_tensor(f"ps{i}", [128, 512], F32)) for i in range(8)]

        CON = E(nc.semaphore("CON"))
        XNL = E(nc.semaphore("XNL"))
        GA = [E(nc.semaphore(f"GA{i}")) for i in range(2)]
        GB = [E(nc.semaphore(f"GB{i}")) for i in range(2)]
        GWC = E(nc.semaphore("GWC"))
        SGS = E(nc.semaphore("SGS"))
        HEV = E(nc.semaphore("HEV"))
        HST = E(nc.semaphore("HST"))
        HLL = E(nc.semaphore("HLL"))
        ATL = E(nc.semaphore("ATL"))
        DW = [E(nc.semaphore(f"DW{i}")) for i in range(2)]
        DWC = E(nc.semaphore("DWC"))
        X1L = E(nc.semaphore("X1L"))
        EVF = E(nc.semaphore("EVF"))
        X2S = E(nc.semaphore("X2S"))
        PPV = E(nc.semaphore("PPV"))
        block = E(nc.Block())

        @block.gpsimd
        def _(g):
            for c in range(32):
                g.dma_start(out=xn[:, c * TPC:(c + 1) * TPC],
                            in_=xn1T[c * 128:(c + 1) * 128, :]).then_inc(XNL, 16)
            for e in range(4 * NGRP):
                g.wait_ge(HEV, e + 1)
                grp, r = divmod(e, 4)
                pair, th = divmod(r, 2)
                ic = grp * 2 + pair
                g.dma_start(out=hdr[ic * 128:(ic + 1) * 128, th * 512:(th + 1) * 512],
                            in_=hst[:, (e % 4) * 512:(e % 4 + 1) * 512]).then_inc(HST, 16)
            for th in range(2):
                if th == 0:
                    g.wait_ge(HST, 16 * 4 * NGRP)
                else:
                    g.wait_ge(X2S, 16 * 32)
                g.dma_start(out=hbuf[:, :].rearrange("p (c w) -> p c w", w=512),
                            in_=hdr[:, th * 512:(th + 1) * 512]
                            .rearrange("(c p) w -> p c w", p=128)).then_inc(HLL, 16)
                g.dma_start(out=attb[:, :].rearrange("p (c w) -> p c w", w=512),
                            in_=attT[:, th * 512:(th + 1) * 512]
                            .rearrange("(c p) w -> p c w", p=128)).then_inc(ATL, 16)
                for ng in range(8):
                    for cc in range(4):
                        c = ng * 4 + cc
                        idx = th * 32 + c
                        if idx >= 4:
                            g.wait_ge(EVF, idx - 3)
                        g.dma_start(out=x1t[:, (idx % 4) * 512:(idx % 4 + 1) * 512],
                                    in_=x1T[c * 128:(c + 1) * 128,
                                            th * 512:(th + 1) * 512]).then_inc(X1L, 16)
            g.wait_ge(X2S, 16 * 64)

        @block.sync
        def _(sync):
            for grp in range(NGRP):
                for d in range(32):
                    i = grp * 32 + d
                    slot = i % 2
                    if i >= 2:
                        sync.wait_ge(GWC, i - 1)
                    sync.dma_start_transpose(
                        gsl[:, slot * 256:(slot + 1) * 256],
                        guT[grp * 256:(grp + 1) * 256,
                            d * 128:(d + 1) * 128]).then_inc(GA[slot], 16)
                    sync.dma_start_transpose(
                        usl[:, slot * 256:(slot + 1) * 256],
                        guT[INTER + grp * 256:INTER + (grp + 1) * 256,
                            d * 128:(d + 1) * 128]).then_inc(GB[slot], 16)
            for th in range(2):
                for ng in range(8):
                    for j in range(118):
                        i = (th * 8 + ng) * 118 + j
                        slot = i % 2
                        if i >= 2:
                            sync.wait_ge(DWC, i - 1)
                        if j < 32:
                            sync.dma_start_transpose(
                                wsl[:, slot * 512:(slot + 1) * 512],
                                wo[ng * 512:(ng + 1) * 512,
                                   j * 128:(j + 1) * 128]).then_inc(DW[slot], 16)
                        else:
                            i2 = j - 32
                            sync.dma_start_transpose(
                                wsl[:, slot * 512:(slot + 1) * 512],
                                dnT[ng * 512:(ng + 1) * 512,
                                    i2 * 128:(i2 + 1) * 128]).then_inc(DW[slot], 16)

        @block.scalar
        def _(sc):
            for e in range(4 * NGRP):
                grp, r = divmod(e, 4)
                pair, th = divmod(r, 2)
                gbank = pair * 4 + th
                sc.wait_ge(GWC, 32 * (grp + 1))
                if e >= 2:
                    sc.wait_ge(HEV, e - 1)
                sc.activation(sg[:, (e % 2) * 512:(e % 2 + 1) * 512],
                              ps[gbank][:], AF.Sigmoid).then_inc(SGS, 1)

        @block.vector
        def _(v):
            for e in range(4 * NGRP):
                grp, r = divmod(e, 4)
                pair, th = divmod(r, 2)
                gbank = pair * 4 + th
                ubank = pair * 4 + 2 + th
                v.wait_ge(SGS, e + 1)
                v.tensor_mul(gsb[:], ps[gbank][:],
                             sg[:, (e % 2) * 512:(e % 2 + 1) * 512]).then_inc(PPV, 1)
                v.wait_ge(PPV, e + 1)
                if e >= 4:
                    v.wait_ge(HST, 16 * (e - 3))
                v.tensor_mul(hst[:, (e % 4) * 512:(e % 4 + 1) * 512],
                             ps[ubank][:], gsb[:]).then_inc(HEV, 1)
            for th in range(2):
                for ng in range(8):
                    for cc in range(4):
                        idx = th * 32 + ng * 4 + cc
                        v.wait_ge(DWC, (th * 8 + ng + 1) * 118)
                        v.wait_ge(X1L, 16 * (idx + 1))
                        if idx >= 2:
                            v.wait_ge(X2S, 16 * (idx - 1))
                        v.scalar_tensor_tensor(
                            x2st[:, (idx % 2) * 512:(idx % 2 + 1) * 512],
                            ps[cc][:], 1.0, x1t[:, (idx % 4) * 512:(idx % 4 + 1) * 512],
                            ALU.mult, ALU.add).then_inc(EVF, 1)

        @block.tensor
        def _(t_):
            t_.wait_ge(XNL, 16 * 32)
            for grp in range(NGRP):
                if grp > 0:
                    t_.wait_ge(HEV, 4 * grp)
                for d in range(32):
                    i = grp * 32 + d
                    slot = i % 2
                    t_.wait_ge(GA[slot], 16 * (i // 2 + 1))
                    t_.wait_ge(GB[slot], 16 * (i // 2 + 1))
                    for pair in range(2):
                        for th in range(2):
                            t_.matmul(ps[pair * 4 + th][:],
                                      gsl[:, slot * 256 + pair * 128:slot * 256 + (pair + 1) * 128],
                                      xn[:, d * TPC + th * 512:d * TPC + (th + 1) * 512],
                                      start=(d == 0), stop=(d == 31))
                            mmf = t_.matmul(ps[pair * 4 + 2 + th][:],
                                      usl[:, slot * 256 + pair * 128:slot * 256 + (pair + 1) * 128],
                                      xn[:, d * TPC + th * 512:d * TPC + (th + 1) * 512],
                                      start=(d == 0), stop=(d == 31))
                    mmf.then_inc(GWC, 1)
            for th in range(2):
                t_.wait_ge(HLL, 16 * (th + 1))
                t_.wait_ge(ATL, 16 * (th + 1))
                for ng in range(8):
                    if th > 0 or ng > 0:
                        t_.wait_ge(EVF, th * 32 + ng * 4)
                    for j in range(118):
                        i = (th * 8 + ng) * 118 + j
                        slot = i % 2
                        t_.wait_ge(DW[slot], 16 * (i // 2 + 1))
                        for cc in range(4):
                            rhs = (attb if j < 32 else hbuf)
                            rix = j if j < 32 else (j - 32)
                            mmf = t_.matmul(ps[cc][:],
                                      wsl[:, slot * 512 + cc * 128:slot * 512 + (cc + 1) * 128],
                                      rhs[:, rix * 512:(rix + 1) * 512],
                                      start=(j == 0), stop=(j == 117))
                        mmf.then_inc(DWC, 1)

        @block.scalar
        def _(sc2):
            for th in range(2):
                for ng in range(8):
                    for cc in range(4):
                        idx = th * 32 + ng * 4 + cc
                        c = ng * 4 + cc
                        sc2.wait_ge(EVF, idx + 1)
                        sc2.dma_start(out=x2T[c * 128:(c + 1) * 128,
                                              th * 512:(th + 1) * 512],
                                      in_=x2st[:, (idx % 2) * 512:(idx % 2 + 1) * 512]).then_inc(X2S, 16)
            sc2.wait_ge(X2S, 16 * 64)

    return nc



def _sl(a, c, n):
    return np.ascontiguousarray(a[c * n:(c + 1) * n])


def run_convert(inp):
    nc = build_convert()
    in_maps = []
    q0 = np.ascontiguousarray(inp['L0_ffn_qweight'].reshape(D, D))
    s0 = np.ascontiguousarray(inp['L0_ffn_scales'].reshape(D, 32))
    z0 = np.ascontiguousarray(inp['L0_ffn_scaled_zeros'].reshape(D, 32))
    counts = [11, 11, 11, 11, 11, 11, 10, 10]
    starts = np.cumsum([0] + counts)[:-1]
    qdn = inp['L1_ffn_qweight_dn']
    zdn = inp['L1_ffn_qzeros_dn']
    sdn = inp['L1_ffn_scales_dn']
    for c in range(NCORES):
        g0, gn = starts[c], counts[c]
        qd = np.zeros((DN_GPC * 128, 512), dtype=np.int32)
        zd = np.zeros((DN_GPC, 512), dtype=np.int32)
        sd = np.zeros((DN_GPC, D), dtype=np.float32)
        qd[:gn * 128] = qdn[g0 * 128:(g0 + gn) * 128]
        zd[:gn] = zdn[g0:g0 + gn]
        sd[:gn] = sdn[g0:g0 + gn]
        in_maps.append(dict(
            w0s=_sl(inp['L0_attn_weight'], c, 512),
            masks=_sl(inp['L0_attn_mask'], c, 512),
            q0s=_sl(q0, c, 512), s0s=_sl(s0, c, 512), z0s=_sl(z0, c, 512),
            wqs=_sl(inp['L1_attn_Wq'], c, 512),
            wks=_sl(inp['L1_attn_Wk'], c, 128),
            wvs=_sl(inp['L1_attn_Wv'], c, 128),
            wos=_sl(inp['L1_attn_Wo'], c, 512),
            qgus=_sl(inp['L1_ffn_qweight_gu'], c, 512),
            zgus=np.ascontiguousarray(inp['L1_ffn_qzeros_gu'][4 * c:4 * c + 4]),
            sgus=np.ascontiguousarray(inp['L1_ffn_scales_gu'][4 * c:4 * c + 4]),
            qdns=qd, zdns=zd, sdns=sd,
        ))
    res = run_bass_kernel_spmd(nc, in_maps, core_ids=list(range(NCORES)))
    r = res.results
    out = {}
    for k in ['wm_bf', 'wf0_bf', 'wq_bf', 'wo_bf']:
        out[k] = np.concatenate([r[c][k] for c in range(NCORES)], axis=0)
    for k in ['wk_bf', 'wv_bf']:
        out[k] = np.concatenate([r[c][k] for c in range(NCORES)], axis=0)
    out['guT_bf'] = np.concatenate([r[c]['guT_bf'] for c in range(NCORES)], axis=1)
    out['dnT_bf'] = np.concatenate(
        [r[c]['dnT_bf'][:, :counts[c] * 128] for c in range(NCORES)], axis=1)
    return out


def kernel(**inputs):
    conv = run_convert(inputs)
    l2res = run_block0qkv(inputs, conv)
    l2res = [{k: np.asarray(v) for k, v in r.items()} for r in l2res]
    l3res = run_attention(l2res)
    l3res = [{k: np.asarray(v) for k, v in r.items()} for r in l3res]
    l4res = run_final(l2res, l3res, conv)
    out = np.concatenate([np.asarray(r['x2T']).T for r in l4res], axis=0)
    return np.ascontiguousarray(out.reshape(B, S, D))

def run_block0qkv(inp, conv):
    nc = build_block0qkv()
    x = np.asarray(inp['L0_attn_data'], dtype=np.float32).reshape(B * S, D)
    b0r = np.ascontiguousarray(inp['L0_ffn_bias'].astype(np.float32).reshape(32, 128).T)
    in_maps = []
    for c in range(NCORES):
        xsh = np.ascontiguousarray(x[c * TPC:(c + 1) * TPC].T)
        p0 = (c % 4) * TPC
        posv = np.arange(p0, p0 + TPC, dtype=np.int32).reshape(1, TPC)
        in_maps.append(dict(
            xT=xsh, wm=conv['wm_bf'], wf=conv['wf0_bf'], b0r=b0r,
            wq=conv['wq_bf'], wkk=conv['wk_bf'], wvv=conv['wv_bf'], pos=posv))
    res = run_bass_kernel_spmd(nc, in_maps, core_ids=list(range(NCORES)))
    return res.results

def run_attention(l2res):
    nc = build_attention()
    kfull = []
    vfull = []
    for b in range(2):
        ks = np.concatenate([l2res[4 * b + i]['kT'] for i in range(4)], axis=1)
        kfull.append(ks)
        vs = np.concatenate([l2res[4 * b + i]['vtok'] for i in range(4)], axis=0)
        vfull.append(vs)
    in_maps = []
    for c in range(NCORES):
        m = {}
        for u in range(16):
            p = P_SEQ[u]
            b = 0 if u < 8 else 1
            cp = 4 * b + p // 2
            qsl_ = l2res[cp]['qT'][512 * c:512 * (c + 1),
                                   (p % 2) * 512:(p % 2 + 1) * 512]
            m[f'q{u}'] = np.ascontiguousarray(
                np.asarray(qsl_).reshape(4, 128, 512).transpose(1, 0, 2).reshape(128, 2048))
            m[f'k{u}'] = np.ascontiguousarray(kfull[b][128 * c:128 * (c + 1), :512 * (p + 1)])
            vt = vfull[b][:512 * (p + 1), 128 * c:128 * (c + 1)]
            m[f'v{u}'] = np.ascontiguousarray(
                np.asarray(vt).reshape(-1, 128, 128).transpose(1, 0, 2).reshape(128, -1))
            assert m[f'v{u}'].shape == (128, 512 * (p + 1))
        in_maps.append(m)
    res = run_bass_kernel_spmd(nc, in_maps, core_ids=list(range(NCORES)))
    return res.results

U0 = {p: P_SEQ.index(p) for p in range(8)}

def run_final(l2res, l3res, conv):
    nc = build_final()
    in_maps = []
    for c in range(NCORES):
        b = c // 4
        attc = np.zeros((D, TPC), dtype=bf16)
        for kvh in range(NCORES):
            for pi, p in enumerate((2 * (c % 4), 2 * (c % 4) + 1)):
                u = U0[p] + (8 if b == 1 else 0)
                blk = np.asarray(l3res[kvh][f'o{u}'])
                for qh in range(4):
                    attc[512 * kvh + 128 * qh:512 * kvh + 128 * (qh + 1),
                         pi * 512:(pi + 1) * 512] = blk[:, qh * 512:(qh + 1) * 512]
        in_maps.append(dict(
            x1T=np.asarray(l2res[c]['x1T']), xn1T=np.asarray(l2res[c]['xn1T']),
            attT=attc, wo=conv['wo_bf'], guT=conv['guT_bf'],
            dnT=conv['dnT_bf']))
    res = run_bass_kernel_spmd(nc, in_maps, core_ids=list(range(NCORES)))
    return res.results

